# version 07
# baseline (speedup 1.0000x reference)
import contextlib

import numpy as np

import concourse.bass as bass
import concourse.mybir as mybir
from concourse.bass_utils import run_bass_kernel_spmd

F32 = mybir.dt.float32
BF16 = mybir.dt.bfloat16
AF = mybir.ActivationFunctionType
ALU = mybir.AluOpType
AX = mybir.AxisListType

D = 2048
S = 4096
B = 4
NH = 16
DK = 128
DFF = 5461
NDC = 43
DFFP = NDC * 128
E = 8
NTOK = B * S
KSEL = NTOK // E
NB = 8
TB = 512
NFI = 16
NQB = 4
SPANS = [8, 16, 24, 32]
OWN_BLOCKS = ([0, 3, 5, 6], [1, 2, 4, 7])
SOFTMAX_SHIFT = -15.0
INV_SQRT_DK = float(np.float32(1.0) / np.sqrt(np.float32(DK)))
N_BISECT = 50
LN_EPS = 1e-5
N_CORES = 8

ENGS = ("tensor", "vector", "scalar", "gpsimd", "sync")

DEBUG_TAPS = []



_CONST = {}


def _host_constants():
    if _CONST:
        return _CONST
    import jax
    import jax.numpy as jnp

    cpu = jax.devices("cpu")[0]
    with jax.default_device(cpu):
        pos = jnp.arange(S, dtype=jnp.float32)[:, None]
        inv = jnp.float32(10000.0) ** (-(jnp.arange(0, DK, 2, dtype=jnp.float32) / DK))
        ang = pos * inv
        _CONST["sin"] = np.asarray(jnp.repeat(jnp.sin(ang), 2, axis=1), np.float32)
        _CONST["cos"] = np.asarray(jnp.repeat(jnp.cos(ang), 2, axis=1), np.float32)
        u = jax.random.uniform(jax.random.key(42, impl="threefry2x32"),
                               (NTOK, E), dtype=jnp.float32)
        _CONST["gumbel"] = np.asarray(-jnp.log(-jnp.log(u + 1e-10) + 1e-10), np.float32)
    P = np.zeros((DK, DK), dtype=np.float32)
    for i in range(0, DK, 2):
        P[i, i + 1] = -1.0
        P[i + 1, i] = 1.0
    _CONST["permT"] = np.ascontiguousarray(P.T)
    _CONST["ident"] = np.eye(128, dtype=np.float32)
    return _CONST


def _attn_masks(own_blocks):
    m = np.zeros((NQB, 8, 128, TB), dtype=np.float32)
    tk = np.arange(128)[:, None]
    tq = np.arange(TB)[None, :]
    for qbi, bi in enumerate(own_blocks):
        q0 = bi * TB
        for slot in range(8):
            k0 = (SPANS[qbi] - 8 + slot) * 128
            m[qbi, slot] = ((k0 + tk) <= (q0 + tq)).astype(np.float32)
    return m



class Prog:
    def __init__(self, nc):
        self.nc = nc
        self.q = {e: [] for e in ENGS}
        self.cnt = {}
        self.sem = {}

    def new_sem(self, es, name):
        self.sem[name] = es.enter_context(self.nc.semaphore(name))
        self.cnt[name] = 0

    def wait(self, eng, sem, val=None):
        v = self.cnt[sem] if val is None else val
        if v <= 0:
            return v
        h = self.sem[sem]
        self.q[eng].append(lambda e, h=h, v=v: e.wait_ge(h, v))
        return v

    def do(self, eng, fn, inc=None, n=1):
        if inc is None:
            self.q[eng].append(fn)
            return None
        h = self.sem[inc]
        self.cnt[inc] += n
        self.q[eng].append(lambda e, h=h, n=n, fn=fn: fn(e).then_inc(h, n))
        return self.cnt[inc]

    def vfence(self, inc=None):
        self.q["vector"].append(lambda e: e.nop(cycle_cnt=2400, nofuse=True))
        if inc is None:
            return None
        h = self.sem[inc]
        self.cnt[inc] += 1
        dmy = self.dummy_ap
        self.q["vector"].append(lambda e, h=h, d=dmy: e.memset(d, 0.0).then_inc(h, 1))
        return self.cnt[inc]

    def dma(self, eng, sem, out_ap, in_ap):
        h = self.sem[sem]
        self.cnt[sem] += 16
        self.q[eng].append(
            lambda e, h=h, o=out_ap, i=in_ap: e.dma_start(out=o, in_=i).then_inc(h, 16))
        return self.cnt[sem]


def _bcast_free(ap, count, pos=1):
    dims = list(ap.ap)
    dims.insert(pos, [0, count])
    return bass.AP(ap.tensor, ap.offset, dims)



def build_program():
    nc = bass.Bass(target_bir_lowering=False)
    dr = {}

    def din(name, shape, dt=F32):
        dr[name] = nc.dram_tensor(name, shape, dt, kind="ExternalInput")

    def scr(name, shape, dt):
        dr[name] = nc.dram_tensor(name, shape, dt)

    din("xt_full", [D, S])
    din("xt_own", [D, NQB * TB])
    din("wqT", [D, D]); din("wkT", [D, D]); din("wvT", [D, D]); din("woT", [D, D])
    din("wg", [D, DFFP]); din("wu", [D, DFFP]); din("wd", [DFFP, D])
    din("bg", [128, NDC]); din("bu", [128, NDC]); din("bd", [128, NFI])
    din("rw", [D, E]); din("rb", [E, 1])
    din("cos_k", [DK, S]); din("sin_k", [DK, S])
    din("cos_q", [DK, NQB * TB]); din("sin_q", [DK, NQB * TB])
    din("gumbel_own", [E, NQB * TB])
    din("masks", [NQB, 8, 128, TB])
    din("permT", [DK, DK])
    din("ident", [128, 128])
    din("ones_col", [128, 1])
    din("ones_row", [1, 128])

    out_t = nc.dram_tensor("out_fm", [NQB, NFI, 128, TB], F32, kind="ExternalOutput")
    _tap_decl = []

    for t_ in ("h", "l"):
        scr(f"wq_{t_}", [NFI, 128, NH * 128], BF16)
        scr(f"wk_{t_}", [NFI, 128, NH * 128], BF16)
        scr(f"wv_{t_}", [NFI, 128, 4 * 512], BF16)
        scr(f"wo_{t_}", [NFI, 128, NFI * 128], BF16)
        scr(f"wg_{t_}", [NFI, 128, DFFP], BF16)
        scr(f"wu_{t_}", [NFI, 128, DFFP], BF16)
        scr(f"wd_{t_}", [NFI, 128, NDC, 128], BF16)
        scr(f"k_{t_}", [NH, DK, S], BF16)
        scr(f"q_{t_}", [NH, DK, NQB * TB], BF16)
        scr(f"v_{t_}", [NH, 32, 128, DK], BF16)
        scr(f"hbuf_{t_}", [NQB, 128, NDC, TB], BF16)
    scr("stream", [NQB, NFI, 128, TB], F32)
    scr("ln2_rows", [2, NQB * TB], F32)
    scr("srow_tmp", [NFI, 128], F32)
    scr("cc_in", [NQB * TB, E], F32)
    scr("cc_out", [NTOK, E], F32)

    sb = {}

    SB_BASE = 16512

    def sbt(name, shape, dt, off):
        sb[name] = nc.alloc_sbuf_tensor_at(name, shape, dt, offset=SB_BASE + off)

    sbt("ones_col", [128, 1], F32, 0)
    sbt("ones_bf", [128, 1], BF16, 32)
    sbt("ones_row", [1, 128], F32, 64)
    sbt("ident", [128, 128], F32, 576)
    sbt("permT", [DK, DK], BF16, 1088)
    sbt("rw_h", [128, NFI, E], BF16, 1344)
    sbt("rw_l", [128, NFI, E], BF16, 1600)
    sbt("rb", [E, 1], F32, 1856)
    sbt("bg", [128, NDC], F32, 1888)
    sbt("bu", [128, NDC], F32, 2080)
    sbt("bd", [128, NFI], F32, 2272)
    sbt("pbounce", [128, 128], F32, 2368)
    sbt("vdummy", [128, 1], F32, 2944)
    sbt("c_shift", [128, 1], F32, 2880)
    sbt("c_scale", [128, 1], F32, 2912)
    PH = 2976

    sbt("xblk", [128, NFI, TB], F32, PH + 0)
    sbt("x1h", [128, NFI, TB], BF16, PH + 32768)
    sbt("x1l", [128, NFI, TB], BF16, PH + 49152)
    sbt("nrm", [128, TB], F32, PH + 65536)
    sbt("wqk_h0", [128, NFI, 128], BF16, PH + 67584)
    sbt("wqk_l0", [128, NFI, 128], BF16, PH + 71680)
    sbt("wqk_h1", [128, NFI, 128], BF16, PH + 75776)
    sbt("wqk_l1", [128, NFI, 128], BF16, PH + 79872)
    sbt("wv_hs", [128, NFI, 512], BF16, PH + 83968)
    sbt("wv_ls", [128, NFI, 512], BF16, PH + 100352)
    sbt("kf", [128, TB], F32, PH + 116736)
    sbt("kh0", [128, TB], BF16, PH + 118784)
    sbt("kl0", [128, TB], BF16, PH + 119808)
    sbt("rope_t1", [128, TB], F32, PH + 120832)
    sbt("krh", [128, TB], BF16, PH + 122880)
    sbt("krl", [128, TB], BF16, PH + 123904)
    sbt("cos_s", [128, TB], F32, PH + 124928)
    sbt("sin_s", [128, TB], F32, PH + 126976)
    sbt("vh_ev", [128, 512], BF16, PH + 129024)
    sbt("vl_ev", [128, 512], BF16, PH + 130048)
    sbt("stat_row", [1, 4 * TB], F32, PH + 131072)
    sbt("spl_in0", [128, 2048], F32, PH + 139264)
    sbt("spl_in1", [128, 2048], F32, PH + 147456)
    sbt("spl_h", [128, 2048], BF16, PH + 155648)
    sbt("spl_l", [128, 2048], BF16, PH + 159744)

    sbt("ksp_h0", [128, 32 * 128], BF16, PH + 0)
    sbt("ksp_l0", [128, 32 * 128], BF16, PH + 8192)
    sbt("vsp_h0", [128, 32 * DK], BF16, PH + 16384)
    sbt("vsp_l0", [128, 32 * DK], BF16, PH + 24576)
    sbt("ksp_h1", [128, 32 * 128], BF16, PH + 32768)
    sbt("ksp_l1", [128, 32 * 128], BF16, PH + 40960)
    sbt("vsp_h1", [128, 32 * DK], BF16, PH + 49152)
    sbt("vsp_l1", [128, 32 * DK], BF16, PH + 57344)
    sbt("qh0", [128, TB], BF16, PH + 65536)
    sbt("ql0", [128, TB], BF16, PH + 66560)
    sbt("qh1", [128, TB], BF16, PH + 67584)
    sbt("ql1", [128, TB], BF16, PH + 68608)
    sbt("obuf_h", [128, NFI, TB], BF16, PH + 69632)
    sbt("obuf_l", [128, NFI, TB], BF16, PH + 86016)
    sbt("mask_sb", [128, 8, TB], F32, PH + 102400)
    sbt("pf0", [128, TB], F32, PH + 118784)
    sbt("pf1", [128, TB], F32, PH + 120832)
    sbt("ph0", [128, TB], BF16, PH + 122880)
    sbt("ph1", [128, TB], BF16, PH + 123904)
    sbt("pl0", [128, TB], BF16, PH + 124928)
    sbt("pl1", [128, TB], BF16, PH + 125952)
    sbt("rs_row", [1, TB], F32, PH + 126976)
    sbt("rc_row", [1, TB], F32, PH + 129024)
    sbt("bc_sb", [128, TB], F32, PH + 131072)
    sbt("of32", [128, TB], F32, PH + 133120)
    sbt("wo_h0", [128, NFI, 128], BF16, PH + 135168)
    sbt("wo_l0", [128, NFI, 128], BF16, PH + 139264)
    sbt("wo_h1", [128, NFI, 128], BF16, PH + 143360)
    sbt("wo_l1", [128, NFI, 128], BF16, PH + 147456)
    sbt("xo0", [128, TB], F32, PH + 151552)
    sbt("xo1", [128, TB], F32, PH + 153600)
    sbt("zt0", [128, TB], F32, PH + 155648)
    sbt("zt1", [128, TB], F32, PH + 157696)
    sbt("sq0", [128, TB], F32, PH + 159744)
    sbt("sq1", [128, TB], F32, PH + 161792)

    sbt("x2h", [128, NFI, NQB * TB], BF16, PH + 0)
    sbt("x2l", [128, NFI, NQB * TB], BF16, PH + 65536)
    sbt("gum", [E, NQB * TB], F32, PH + 131072)
    sbt("smg", [128, E * 128], F32, PH + 139264)
    sbt("bis", [128, E * 128], F32, PH + 143360)
    sbt("sm_own", [128, NFI, E], F32, PH + 147456)
    sbt("smt1", [128, E], F32, PH + 147968)
    sbt("smt2", [128, E], F32, PH + 148000)
    sbt("smt3", [128, 1], F32, PH + 148032)
    sbt("smt4", [128, 1], F32, PH + 150880)
    sbt("gmask", [128, E], mybir.dt.uint32, PH + 150912)
    sbt("lohi", [128, 3 * E], F32, PH + 148064)
    sbt("cnt_pe", [128, NFI], F32, PH + 148160)
    sbt("cnt_row", [1, E], F32, PH + 148224)
    sbt("hrow", [E, TB], F32, PH + 148256)
    sbt("brows", [1, 8, 512], F32, PH + 151584)
    sbt("smg_raw", [128, 128, E], F32, PH + 150304)
    sbt("s16", [16, 128], F32, PH + 154400)
    sbt("wf_gh0", [128, NFI, 128], BF16, PH + 131072)
    sbt("wf_gl0", [128, NFI, 128], BF16, PH + 135168)
    sbt("wf_uh0", [128, NFI, 128], BF16, PH + 139264)
    sbt("wf_ul0", [128, NFI, 128], BF16, PH + 143360)
    sbt("wf_gh1", [128, NFI, 128], BF16, PH + 147456)
    sbt("wf_gl1", [128, NFI, 128], BF16, PH + 151552)
    sbt("wf_uh1", [128, NFI, 128], BF16, PH + 155648)
    sbt("wf_ul1", [128, NFI, 128], BF16, PH + 159744)
    sbt("s_row", [1, NQB * TB], F32, PH + 163840)
    sbt("row_a", [1, NQB * TB], F32, PH + 172032)
    sbt("row_b", [1, NQB * TB], F32, PH + 180224)
    sbt("gev", [128, TB], F32, PH + 188416)
    sbt("uev", [128, TB], F32, PH + 190464)
    sbt("hfv", [128, TB], F32, PH + 192512)
    sbt("hh_ev", [128, TB], BF16, PH + 194560)
    sbt("hl_ev", [128, TB], BF16, PH + 195584)
    sbt("strm_t", [128, TB], F32, PH + 196608)

    sbt("hsp_h", [128, NDC, TB], BF16, PH + 0)
    sbt("hsp_l", [128, NDC, TB], BF16, PH + 44032)
    sbt("wd_h0", [128, 22, 128], BF16, PH + 88064)
    sbt("wd_l0", [128, 22, 128], BF16, PH + 93696)
    sbt("wd_h1", [128, 22, 128], BF16, PH + 99328)
    sbt("wd_l1", [128, 22, 128], BF16, PH + 104960)
    sbt("yt0", [128, TB], F32, PH + 110592)
    sbt("yt1", [128, TB], F32, PH + 112640)
    sbt("s_bc", [128, TB], F32, PH + 114688)
    sbt("st0", [128, TB], F32, PH + 116736)
    sbt("st1", [128, TB], F32, PH + 118784)
    sbt("ot0", [128, TB], F32, PH + 120832)
    sbt("ot1", [128, TB], F32, PH + 122880)
    sbt("y_part", [128, NFI, TB], F32, PH + 124928)

    for _name in DEBUG_TAPS:
        if _name.startswith("sb_"):
            continue
        h_ = dr[_name]
        dr["dbg_" + _name] = nc.dram_tensor("dbg_" + _name, list(h_.shape), h_.dtype,
                                            kind="ExternalOutput")

    pbk = [nc.alloc_psum_tensor(f"pb{i}", [128, 512], F32) for i in range(8)]

    es = contextlib.ExitStack()
    with es:
        P = Prog(nc)
        P.dummy_ap = sb["vdummy"].ap()
        for s_ in ("dS", "dG", "dA", "eP", "eV", "eA", "cc"):
            P.new_sem(es, s_)
        block = es.enter_context(nc.Block())

        for _name in DEBUG_TAPS:
            if _name.startswith("sb_"):
                h_ = sb[_name[3:]]
                dr["dbg_" + _name] = nc.dram_tensor("dbg_" + _name, list(h_.shape),
                                                    h_.dtype, kind="ExternalOutput")
        d = {k: v.ap() for k, v in dr.items()}
        t = {k: v.ap() for k, v in sb.items()}
        _phases(P, d, t, pbk, out_t.ap(), dr)

        for eng in ENGS:
            def mk(eng):
                def run(e):
                    for fn in P.q[eng]:
                        fn(e)
                return run
            getattr(block, eng)(mk(eng))
    return nc



def _barrier(P):
    for eng in ENGS:
        for s_ in ("dS", "dG", "dA", "eP", "eV", "eA"):
            P.wait(eng, s_)


def _mm3(P, lh, ll, rh, rl, out, first, last, inc=True):
    P.do("tensor", lambda e, lh=lh, rh=rh, o=out, f=first: e.matmul(o, lh, rh, start=f, stop=False))
    P.do("tensor", lambda e, lh=lh, rl=rl, o=out: e.matmul(o, lh, rl, start=False, stop=False))
    return P.do("tensor", lambda e, ll=ll, rh=rh, o=out, l=last: e.matmul(o, ll, rh, start=False, stop=l),
                inc=("eP" if (last and inc) else None))


def _phases(P, d, t, pbk, out_ap, dr):
    st = {}

    for name, src in (("ones_col", "ones_col"), ("ones_row", "ones_row"),
                      ("ident", "ident"), ("rb", "rb"), ("bg", "bg"),
                      ("bu", "bu"), ("bd", "bd")):
        P.dma("gpsimd", "dG", t[name][:], d[src])
    vperm = P.dma("gpsimd", "dG", t["pbounce"][:], d["permT"])
    P.do("vector", lambda e: e.memset(t["c_shift"][:], SOFTMAX_SHIFT))
    P.do("vector", lambda e: e.memset(t["c_scale"][:], INV_SQRT_DK))
    P.wait("vector", "dG", vperm)
    P.do("vector", lambda e: e.tensor_copy(t["permT"][:], t["pbounce"][:]))
    vcast = P.do("vector", lambda e: e.tensor_copy(t["ones_bf"][:], t["ones_col"][:]), inc="eV")

    spl = {"buf": 0, "free": [(0, 0), (0, 0)], "h_store": 0, "l_store": 0,
           "h_store3": 0}

    def split_chunk(src_ap, cols, dst_h, dst_l, d3=False):
        bi = spl["buf"] % 2
        spl["buf"] += 1
        inb = t["spl_in0"] if bi == 0 else t["spl_in1"]
        fa, fv = spl["free"][bi]
        P.wait("gpsimd", "eA", fa)
        P.wait("gpsimd", "eV", fv)
        v = P.dma("gpsimd", "dG", inb[:, 0:cols], src_ap)
        P.wait("scalar", "dG", v)
        P.wait("scalar", "dA", spl["h_store"])
        P.wait("scalar", "dG", spl["h_store3"])
        va = P.do("scalar", lambda e, i=inb, c=cols: e.activation(
            t["spl_h"][:, 0:c], i[:, 0:c], AF.Copy, bias=0.0, scale=1.0), inc="eA")
        P.wait("vector", "eA", va)
        P.wait("vector", "dG", spl["l_store"])
        vv = P.do("vector", lambda e, i=inb, c=cols: e.tensor_sub(
            t["spl_l"][:, 0:c], i[:, 0:c], t["spl_h"][:, 0:c]), inc="eV")
        spl["free"][bi] = (va, vv)
        if d3:
            ci = dst_h.shape[-1]
            P.wait("gpsimd", "eA", va)
            spl["h_store3"] = P.dma("gpsimd", "dG", dst_h,
                                    t["spl_h"][:, 0:cols].rearrange("p (a c) -> p a c", c=ci))
            P.wait("gpsimd", "eV", vv)
            spl["l_store"] = P.dma("gpsimd", "dG", dst_l,
                                   t["spl_l"][:, 0:cols].rearrange("p (a c) -> p a c", c=ci))
        else:
            P.wait("scalar", "eA", va)
            spl["h_store"] = P.dma("scalar", "dA", dst_h, t["spl_h"][:, 0:cols])
            P.wait("gpsimd", "eV", vv)
            spl["l_store"] = P.dma("gpsimd", "dG", dst_l, t["spl_l"][:, 0:cols])

    def wsplit(src, hi, lo):
        for r in range(NFI):
            split_chunk(d[src][r * 128:(r + 1) * 128, :], D,
                        d[hi][r, :, :], d[lo][r, :, :])
        return (spl["h_store"], spl["l_store"])

    wk_st = wsplit("wkT", "wk_h", "wk_l")
    wv_st = wsplit("wvT", "wv_h", "wv_l")
    wq_st = wsplit("wqT", "wq_h", "wq_l")
    wo_st = wsplit("woT", "wo_h", "wo_l")
    P.wait("gpsimd", "eV", vcast)
    for r in range(NFI):
        v = P.dma("gpsimd", "dG", t["pbounce"][:, 0:E], d["rw"][r * 128:(r + 1) * 128, :])
        P.wait("scalar", "dG", v)
        va = P.do("scalar", lambda e, r=r: e.activation(
            t["rw_h"][:, r, :], t["pbounce"][:, 0:E], AF.Copy, bias=0.0, scale=1.0), inc="eA")
        P.wait("vector", "eA", va)
        vv = P.do("vector", lambda e, r=r: e.tensor_sub(
            t["rw_l"][:, r, :], t["pbounce"][:, 0:E], t["rw_h"][:, r, :]), inc="eV")
        P.wait("gpsimd", "eV", vv)

    ffn_chunks = []
    for srcn, hin, lon in (("wg", "wg_h", "wg_l"), ("wu", "wu_h", "wu_l")):
        for r in range(NFI):
            for c0, cw in ((0, 2048), (2048, 2048), (4096, 1408)):
                src = d[srcn][r * 128:(r + 1) * 128, c0:c0 + cw]
                dh = d[hin][r, :, c0:c0 + cw]
                dl = d[lon][r, :, c0:c0 + cw]
                ffn_chunks.append((src, cw, dh, dl, False))
    for r in range(NDC):
        src = d["wd"][r * 128:(r + 1) * 128, :]
        dh = d["wd_h"][:, :, r, :].rearrange("a p c -> p a c")
        dl = d["wd_l"][:, :, r, :].rearrange("a p c -> p a c")
        ffn_chunks.append((src, D, dh, dl, True))
    ffn_iter = iter(ffn_chunks)

    def more_splits(k):
        for _ in range(k):
            it = next(ffn_iter, None)
            if it is None:
                return
            split_chunk(it[0], it[1], it[2], it[3], it[4])

    st["x_free_P"] = 0
    st["x_free_V"] = 0
    st["prev_x1"] = 0
    st["wqk_free"] = [0, 0]
    st["projbank_ev"] = [0, 0]

    def ln_block(xsrc_col_ap):
        P.wait("sync", "eP", st["x_free_P"])
        P.wait("sync", "eV", st["x_free_V"])
        vx = P.dma("sync", "dS", t["xblk"][:], xsrc_col_ap)
        P.wait("tensor", "dS", vx)
        P.wait("tensor", "eV", st["prev_x1"])
        vsum = 0
        for fi in range(NFI):
            vsum = P.do("tensor", lambda e, fi=fi: e.matmul(
                pbk[4][0:1, 0:TB], t["ones_col"][:, 0:1], t["xblk"][:, fi, :],
                start=(fi == 0), stop=(fi == NFI - 1)), inc="eP")
        P.wait("scalar", "dS", vx)
        vsq = 0
        for fi in range(NFI):
            if fi >= 1:
                P.wait("scalar", "eP", vsq)
            va = P.do("scalar", lambda e, fi=fi: e.activation(
                t["nrm"][:], t["xblk"][:, fi, :], AF.Square, bias=0.0, scale=1.0), inc="eA")
            P.wait("tensor", "eA", va)
            vsq = P.do("tensor", lambda e, fi=fi: e.matmul(
                pbk[5][0:1, 0:TB], t["ones_col"][:, 0:1], t["nrm"][:],
                start=(fi == 0), stop=(fi == NFI - 1)), inc="eP")
        sr = t["stat_row"]
        mean = sr[:, 0:TB]; var = sr[:, TB:2 * TB]
        rstd = sr[:, 2 * TB:3 * TB]; tmp = sr[:, 3 * TB:4 * TB]
        P.wait("vector", "eP", vsq)
        P.do("vector", lambda e: e.tensor_scalar_mul(mean, pbk[4][0:1, 0:TB], 1.0 / D))
        P.do("vector", lambda e: e.tensor_scalar_mul(var, pbk[5][0:1, 0:TB], 1.0 / D))
        P.do("vector", lambda e: e.tensor_mul(tmp, mean, mean))
        P.do("vector", lambda e: e.tensor_sub(var, var, tmp))
        P.do("vector", lambda e: e.tensor_scalar_add(var, var, LN_EPS))
        vrec = P.do("vector", lambda e: e.reciprocal(tmp, var), inc="eV")
        P.wait("scalar", "eV", vrec)
        vrs = P.do("scalar", lambda e: e.activation(rstd, tmp, AF.Sqrt, bias=0.0, scale=1.0),
                   inc="eA")
        P.wait("tensor", "eA", vrs)
        P.do("tensor", lambda e: e.matmul(pbk[6][:, 0:TB], t["ones_row"][:], mean,
                                          start=True, stop=True))
        vrep = P.do("tensor", lambda e: e.matmul(pbk[7][:, 0:TB], t["ones_row"][:], rstd,
                                                 start=True, stop=True), inc="eP")
        P.wait("vector", "eP", vrep)
        for fi in range(NFI):
            P.do("vector", lambda e, fi=fi: e.tensor_sub(t["nrm"][:], t["xblk"][:, fi, :],
                                                         pbk[6][:, 0:TB]))
            P.do("vector", lambda e: e.tensor_mul(t["nrm"][:], t["nrm"][:], pbk[7][:, 0:TB]))
            P.do("vector", lambda e, fi=fi: e.tensor_copy(t["x1h"][:, fi, :], t["nrm"][:]))
            P.do("vector", lambda e, fi=fi: e.tensor_sub(t["x1l"][:, fi, :], t["nrm"][:],
                                                         t["x1h"][:, fi, :]),
                 inc=("eV" if fi == NFI - 1 else None))
        st["prev_x1"] = P.cnt["eV"]
        st["x_free_P"] = vsum
        st["x_free_V"] = st["prev_x1"]
        return st["prev_x1"]

    def proj_head(h, wsrc_h, wsrc_l, dst_h, dst_l, v_x1, wst):
        par = h % 2
        wh = t["wqk_h0"] if par == 0 else t["wqk_h1"]
        wl = t["wqk_l0"] if par == 0 else t["wqk_l1"]
        P.wait("sync", "dA", wst[0])
        P.wait("sync", "dG", wst[1])
        P.wait("sync", "eP", st["wqk_free"][par])
        P.dma("sync", "dS", wh[:],
              wsrc_h[:, :, h * 128:(h + 1) * 128].rearrange("a p c -> p a c"))
        vw = P.dma("sync", "dS", wl[:],
                   wsrc_l[:, :, h * 128:(h + 1) * 128].rearrange("a p c -> p a c"))
        bank = pbk[par]
        P.wait("tensor", "dS", vw)
        P.wait("tensor", "eV", v_x1)
        P.wait("tensor", "eA", st["projbank_ev"][par])
        for fi in range(NFI):
            vp = _mm3(P, wh[:, fi, :], wl[:, fi, :],
                      t["x1h"][:, fi, :], t["x1l"][:, fi, :],
                      bank[:, 0:TB], fi == 0, fi == NFI - 1)
        st["wqk_free"][par] = vp
        P.wait("scalar", "eP", vp)
        vkf = P.do("scalar", lambda e, b=bank: e.activation(
            t["kf"][:], b[:, 0:TB], AF.Copy, bias=0.0, scale=1.0), inc="eA")
        st["projbank_ev"][par] = vkf
        P.wait("vector", "eA", vkf)
        P.do("vector", lambda e: e.tensor_copy(t["kh0"][:], t["kf"][:]))
        vpair = P.do("vector", lambda e: e.tensor_sub(t["kl0"][:], t["kf"][:], t["kh0"][:]),
                     inc="eV")
        P.wait("tensor", "eV", vpair)
        P.do("tensor", lambda e: e.matmul(pbk[2][:, 0:TB], t["permT"][:], t["kh0"][:],
                                          start=True, stop=False))
        vsw = P.do("tensor", lambda e: e.matmul(pbk[2][:, 0:TB], t["permT"][:], t["kl0"][:],
                                                start=False, stop=True), inc="eP")
        P.wait("vector", "eP", vsw)
        P.do("vector", lambda e: e.tensor_mul(t["rope_t1"][:], t["kf"][:], t["cos_s"][:]))
        P.do("vector", lambda e: e.tensor_mul(t["kf"][:], pbk[2][:, 0:TB], t["sin_s"][:]))
        P.do("vector", lambda e: e.tensor_add(t["rope_t1"][:], t["rope_t1"][:], t["kf"][:]))
        P.do("vector", lambda e: e.tensor_copy(t["krh"][:], t["rope_t1"][:]))
        vr = P.do("vector", lambda e: e.tensor_sub(t["krl"][:], t["rope_t1"][:], t["krh"][:]),
                  inc="eV")
        P.wait("scalar", "eV", vr)
        P.dma("scalar", "dA", dst_h, t["krh"][:])
        P.dma("scalar", "dA", dst_l, t["krl"][:])

    st["vband_ev"] = [0, 0]
    st["wv_free"] = 0

    def v_proj(pb, v_x1):
        for vc in range(4):
            P.wait("sync", "dA", wv_st[0])
            P.wait("sync", "dG", wv_st[1])
            P.wait("sync", "eP", st["wv_free"])
            P.dma("sync", "dS", t["wv_hs"][:],
                  d["wv_h"][:, :, vc * 512:(vc + 1) * 512].rearrange("a p c -> p a c"))
            vw = P.dma("sync", "dS", t["wv_ls"][:],
                       d["wv_l"][:, :, vc * 512:(vc + 1) * 512].rearrange("a p c -> p a c"))
            P.wait("tensor", "dS", vw)
            P.wait("tensor", "eV", v_x1)
            for pt in range(4):
                par = (vc * 4 + pt) % 2
                bank = pbk[par]
                P.wait("tensor", "eA", st["projbank_ev"][par])
                for fi in range(NFI):
                    vv = _mm3(P, t["x1h"][:, fi, pt * 128:(pt + 1) * 128],
                              t["x1l"][:, fi, pt * 128:(pt + 1) * 128],
                              t["wv_hs"][:, fi, :], t["wv_ls"][:, fi, :],
                              bank[:, 0:512], fi == 0, fi == NFI - 1)
                st["wv_free"] = vv
                P.wait("scalar", "eP", vv)
                P.wait("scalar", "dA", st.get("vh_store", 0))
                vvh = P.do("scalar", lambda e, b=bank: e.activation(
                    t["vh_ev"][:], b[:, 0:512], AF.Copy, bias=0.0, scale=1.0), inc="eA")
                st["projbank_ev"][par] = vvh
                P.wait("vector", "eA", vvh)
                P.wait("vector", "dG", st.get("vl_store", 0))
                vvl = P.do("vector", lambda e, b=bank: e.tensor_sub(
                    t["vl_ev"][:], b[:, 0:512], t["vh_ev"][:]), inc="eV")
                gt = pb * 4 + pt
                P.wait("gpsimd", "eV", vvl)
                for hh in range(4):
                    h = vc * 4 + hh
                    st["vh_store"] = P.dma("scalar", "dA", d["v_h"][h, gt, :, :],
                                           t["vh_ev"][:, hh * 128:(hh + 1) * 128])
                    st["vl_store"] = P.dma("gpsimd", "dG", d["v_l"][h, gt, :, :],
                                           t["vl_ev"][:, hh * 128:(hh + 1) * 128])

    xt_full = d["xt_full"].rearrange("(a p) s -> p a s", p=128)
    xt_own = d["xt_own"].rearrange("(a p) s -> p a s", p=128)
    for pb in range(NB):
        v_x1 = ln_block(xt_full[:, :, pb * TB:(pb + 1) * TB])
        P.wait("sync", "eV")
        vtab = P.dma("sync", "dS", t["cos_s"][:], d["cos_k"][:, pb * TB:(pb + 1) * TB])
        vtab = P.dma("sync", "dS", t["sin_s"][:], d["sin_k"][:, pb * TB:(pb + 1) * TB])
        P.wait("vector", "dS", vtab)
        for h in range(NH):
            proj_head(h, d["wk_h"], d["wk_l"],
                      d["k_h"][h, :, pb * TB:(pb + 1) * TB],
                      d["k_l"][h, :, pb * TB:(pb + 1) * TB], v_x1, wk_st)
            more_splits(1)
        v_proj(pb, v_x1)
        more_splits(2)
    for qbi in range(NQB):
        v_x1 = ln_block(xt_own[:, :, qbi * TB:(qbi + 1) * TB])
        P.wait("sync", "eV")
        vtab = P.dma("sync", "dS", t["cos_s"][:], d["cos_q"][:, qbi * TB:(qbi + 1) * TB])
        vtab = P.dma("sync", "dS", t["sin_s"][:], d["sin_q"][:, qbi * TB:(qbi + 1) * TB])
        P.wait("vector", "dS", vtab)
        for h in range(NH):
            proj_head(h, d["wq_h"], d["wq_l"],
                      d["q_h"][h, :, qbi * TB:(qbi + 1) * TB],
                      d["q_l"][h, :, qbi * TB:(qbi + 1) * TB], v_x1, wq_st)
            more_splits(1)
    more_splits(10 ** 9)
    _barrier(P)

    ksp = [(t["ksp_h0"], t["ksp_l0"], t["vsp_h0"], t["vsp_l0"]),
           (t["ksp_h1"], t["ksp_l1"], t["vsp_h1"], t["vsp_l1"])]
    qsp = [(t["qh0"], t["ql0"]), (t["qh1"], t["ql1"])]
    kv_free = [0, 0]
    q_free = [0, 0]
    st["wo_free"] = [0, 0]
    prev_qb_rows = 0

    for qbi in range(NQB):
        span = SPANS[qbi]
        vm = P.dma("sync", "dS", t["mask_sb"][:],
                   d["masks"][qbi, :, :, :].rearrange("a p c -> p a c"))
        P.wait("vector", "dS", vm)
        for h in range(NH):
            slot = h % 2
            kh, kl, vh, vl = ksp[slot]
            qh, ql = qsp[slot]
            P.wait("sync", "eV", kv_free[slot])
            P.dma("sync", "dS", kh[:, 0:span * 128], d["k_h"][h, :, 0:span * 128])
            P.dma("sync", "dS", kl[:, 0:span * 128], d["k_l"][h, :, 0:span * 128])
            P.dma("sync", "dS", vh[:, 0:span * DK].rearrange("p (a c) -> p a c", c=DK),
                  d["v_h"][h].rearrange("a p c -> p a c")[:, 0:span, :])
            P.dma("sync", "dS", vl[:, 0:span * DK].rearrange("p (a c) -> p a c", c=DK),
                  d["v_l"][h].rearrange("a p c -> p a c")[:, 0:span, :])
            P.wait("sync", "eP", q_free[slot])
            P.dma("sync", "dS", qh[:], d["q_h"][h, :, qbi * TB:(qbi + 1) * TB])
            vq = P.dma("sync", "dS", ql[:], d["q_l"][h, :, qbi * TB:(qbi + 1) * TB])
            P.wait("tensor", "dS", vq)
            obank = pbk[2 + (h % 2)]
            rbank = pbk[4 + (h % 2)]
            vsp_s = [0, 0]
            for kt in range(span):
                ps_ = kt % 2
                sbank = pbk[ps_]
                pf = t["pf0"] if ps_ == 0 else t["pf1"]
                ph_ = t["ph0"] if ps_ == 0 else t["ph1"]
                pl_ = t["pl0"] if ps_ == 0 else t["pl1"]
                P.wait("tensor", "eV", vsp_s[ps_])
                _mm3(P, kh[:, kt * 128:(kt + 1) * 128], kl[:, kt * 128:(kt + 1) * 128],
                     qh[:], ql[:], sbank[:, 0:TB], True, True)
                vsc = P.cnt["eP"]
                P.wait("scalar", "eP", vsc)
                vex = P.do("scalar", lambda e, pf=pf, b=sbank: e.activation(
                    pf[:], b[:, 0:TB], AF.Exp, bias=t["c_shift"][:], scale=t["c_scale"][:]),
                    inc="eA")
                P.wait("vector", "eA", vex)
                if kt >= span - 8:
                    sl = kt - (span - 8)
                    P.do("vector", lambda e, pf=pf, sl=sl: e.tensor_mul(
                        pf[:], pf[:], t["mask_sb"][:, sl, :]))
                P.do("vector", lambda e, pf=pf, ph=ph_: e.tensor_copy(ph[:], pf[:]))
                vsp = P.do("vector", lambda e, pf=pf, ph=ph_, pl=pl_: e.tensor_sub(
                    pl[:], pf[:], ph[:]), inc="eV")
                vsp_s[ps_] = vsp
                P.wait("tensor", "eV", vsp)
                first = kt == 0
                last = kt == span - 1
                P.do("tensor", lambda e, vh=vh, ph=ph_, kt=kt, f=first, ob=obank: e.matmul(
                    ob[:, 0:TB], vh[:, kt * DK:(kt + 1) * DK], ph[:], start=f, stop=False))
                P.do("tensor", lambda e, vh=vh, pl=pl_, kt=kt, ob=obank: e.matmul(
                    ob[:, 0:TB], vh[:, kt * DK:(kt + 1) * DK], pl[:], start=False, stop=False))
                P.do("tensor", lambda e, vl=vl, ph=ph_, kt=kt, l=last, ob=obank: e.matmul(
                    ob[:, 0:TB], vl[:, kt * DK:(kt + 1) * DK], ph[:], start=False, stop=l))
                P.do("tensor", lambda e, ph=ph_, f=first, rb=rbank: e.matmul(
                    rb[0:1, 0:TB], t["ones_bf"][:, 0:1], ph[:], start=f, stop=False))
                P.do("tensor", lambda e, pl=pl_, l=last, rb=rbank: e.matmul(
                    rb[0:1, 0:TB], t["ones_bf"][:, 0:1], pl[:], start=False, stop=l), inc="eP")
            vavdone = P.cnt["eP"]
            P.wait("vector", "eP", vavdone)
            vrc = P.do("vector", lambda e, rb=rbank: e.reciprocal(t["rc_row"][:], rb[0:1, 0:TB]),
                       inc="eV")
            P.wait("tensor", "eV", vrc)
            vbc = P.do("tensor", lambda e: e.matmul(pbk[6][:, 0:TB], t["ones_row"][:],
                                                    t["rc_row"][:], start=True, stop=True),
                       inc="eP")
            P.wait("vector", "eP", vbc)
            P.do("vector", lambda e: e.tensor_copy(t["bc_sb"][:], pbk[6][:, 0:TB]))
            P.do("vector", lambda e, ob=obank: e.tensor_mul(t["of32"][:], ob[:, 0:TB],
                                                            t["bc_sb"][:]))
            P.do("vector", lambda e, h=h: e.tensor_copy(t["obuf_h"][:, h, :], t["of32"][:]))
            vob = P.do("vector", lambda e, h=h: e.tensor_sub(
                t["obuf_l"][:, h, :], t["of32"][:], t["obuf_h"][:, h, :]), inc="eV")
            kv_free[slot] = vob
            q_free[slot] = vavdone
        vodone = P.cnt["eV"]

        wo = [(t["wo_h0"], t["wo_l0"]), (t["wo_h1"], t["wo_l1"])]
        xob = [t["xo0"], t["xo1"]]
        ztb = [t["zt0"], t["zt1"]]
        sqb = [t["sq0"], t["sq1"]]
        zt_guard = [0, 0]
        sq_guard = [0, 0]
        xo_guard = [0, 0]
        store_guard = [0, 0]
        vstat_x = 0
        vstat_q = 0
        P.wait("tensor", "eA", prev_qb_rows)
        for fo in range(NFI):
            par = fo % 2
            wh, wl = wo[par]
            P.wait("sync", "dA", wo_st[0])
            P.wait("sync", "dG", wo_st[1])
            P.wait("sync", "eP", st["wo_free"][par])
            P.dma("sync", "dS", wh[:],
                  d["wo_h"][:, :, fo * 128:(fo + 1) * 128].rearrange("a p c -> p a c"))
            vw = P.dma("sync", "dS", wl[:],
                       d["wo_l"][:, :, fo * 128:(fo + 1) * 128].rearrange("a p c -> p a c"))
            P.wait("sync", "eV", xo_guard[par])
            vxo = P.dma("sync", "dS", xob[par][:],
                        xt_own[:, fo, qbi * TB:(qbi + 1) * TB])
            P.wait("tensor", "dS", vxo)
            if fo == 0:
                P.wait("tensor", "eV", vodone)
            bank = pbk[par]
            for vf in range(NFI):
                vz = _mm3(P, wh[:, vf, :], wl[:, vf, :],
                          t["obuf_h"][:, vf, :], t["obuf_l"][:, vf, :],
                          bank[:, 0:TB], vf == 0, vf == NFI - 1)
            st["wo_free"][par] = vz
            P.wait("vector", "eP", vz)
            P.wait("vector", "eP", zt_guard[par])
            P.wait("vector", "dA", store_guard[par])
            vzt = P.do("vector", lambda e, b=bank, z=ztb[par], x=xob[par]: e.tensor_add(
                z[:], b[:, 0:TB], x[:]), inc="eV")
            xo_guard[par] = vzt
            P.wait("scalar", "eV", vzt)
            store_guard[par] = P.dma("scalar", "dA", d["stream"][qbi, fo, :, :], ztb[par][:])
            P.wait("scalar", "eP", sq_guard[par])
            vsq = P.do("scalar", lambda e, z=ztb[par], s=sqb[par]: e.activation(
                s[:], z[:], AF.Square, bias=0.0, scale=1.0), inc="eA")
            P.wait("tensor", "eV", vzt)
            vstat_x = P.do("tensor", lambda e, z=ztb[par], fo=fo: e.matmul(
                pbk[4][0:1, 0:TB], t["ones_col"][:, 0:1], z[:],
                start=(fo == 0), stop=(fo == NFI - 1)), inc="eP")
            zt_guard[par] = vstat_x
            P.wait("tensor", "eA", vsq)
            vstat_q = P.do("tensor", lambda e, s=sqb[par], fo=fo: e.matmul(
                pbk[5][0:1, 0:TB], t["ones_col"][:, 0:1], s[:],
                start=(fo == 0), stop=(fo == NFI - 1)), inc="eP")
            sq_guard[par] = vstat_q
        P.wait("vector", "eP", vstat_q)
        P.do("vector", lambda e: e.tensor_scalar_mul(t["rs_row"][:], pbk[4][0:1, 0:TB], 1.0 / D))
        P.do("vector", lambda e: e.tensor_scalar_mul(t["rc_row"][:], pbk[5][0:1, 0:TB], 1.0 / D))
        P.do("vector", lambda e: e.tensor_mul(t["of32"][0:1, 0:TB], t["rs_row"][:], t["rs_row"][:]))
        P.do("vector", lambda e: e.tensor_sub(t["rc_row"][:], t["rc_row"][:], t["of32"][0:1, 0:TB]))
        P.do("vector", lambda e: e.tensor_scalar_add(t["rc_row"][:], t["rc_row"][:], LN_EPS))
        vv = P.do("vector", lambda e: e.reciprocal(t["rc_row"][:], t["rc_row"][:]), inc="eV")
        P.wait("scalar", "eV", vv)
        vr2 = P.do("scalar", lambda e: e.activation(t["rc_row"][:], t["rc_row"][:], AF.Sqrt,
                                                    bias=0.0, scale=1.0), inc="eA")
        P.wait("scalar", "eA", vr2)
        P.dma("scalar", "dA", d["ln2_rows"][0:1, qbi * TB:(qbi + 1) * TB], t["rs_row"][:])
        prev_qb_rows = P.cnt["eA"]
        P.dma("scalar", "dA", d["ln2_rows"][1:2, qbi * TB:(qbi + 1) * TB], t["rc_row"][:])
    _barrier(P)

    vra = P.dma("sync", "dS", t["row_a"][:], d["ln2_rows"][0:1, :])
    vrb = P.dma("sync", "dS", t["row_b"][:], d["ln2_rows"][1:2, :])
    vgum = P.dma("sync", "dS", t["gum"][:], d["gumbel_own"])
    P.wait("tensor", "dS", vrb)
    prev_norm = 0
    strm_guard = [0, 0]
    for qbi in range(NQB):
        P.wait("tensor", "eV", prev_norm)
        P.do("tensor", lambda e, q=qbi: e.matmul(
            pbk[6][:, 0:TB], t["ones_row"][:], t["row_a"][:, q * TB:(q + 1) * TB],
            start=True, stop=True))
        vrep = P.do("tensor", lambda e, q=qbi: e.matmul(
            pbk[7][:, 0:TB], t["ones_row"][:], t["row_b"][:, q * TB:(q + 1) * TB],
            start=True, stop=True), inc="eP")
        P.wait("vector", "eP", vrep)
        for fi in range(NFI):
            sbuf_ = t["strm_t"] if fi % 2 == 0 else t["gev"]
            P.wait("sync", "eV", strm_guard[fi % 2])
            vld = P.dma("sync", "dS", sbuf_[:], d["stream"][qbi, fi, :, :])
            P.wait("vector", "dS", vld)
            cs = slice(qbi * TB, (qbi + 1) * TB)
            P.do("vector", lambda e, s=sbuf_: e.tensor_sub(t["hfv"][:], s[:], pbk[6][:, 0:TB]))
            P.do("vector", lambda e: e.tensor_mul(t["hfv"][:], t["hfv"][:], pbk[7][:, 0:TB]))
            P.do("vector", lambda e, fi=fi, cs=cs: e.tensor_copy(t["x2h"][:, fi, cs], t["hfv"][:]))
            prev_norm = P.do("vector", lambda e, fi=fi, cs=cs: e.tensor_sub(
                t["x2l"][:, fi, cs], t["hfv"][:], t["x2h"][:, fi, cs]), inc="eV")
            strm_guard[fi % 2] = prev_norm
    v_x2 = prev_norm

    P.wait("vector", "dS", vgum)
    prev_hcopy = 0
    prev_tr = 0
    for qbi in range(NQB):
        cs = slice(qbi * TB, (qbi + 1) * TB)
        P.wait("tensor", "eV", v_x2)
        P.wait("tensor", "eA", prev_hcopy)
        for fi in range(NFI):
            vcl = _mm3(P, t["rw_h"][:, fi, :], t["rw_l"][:, fi, :],
                       t["x2h"][:, fi, cs], t["x2l"][:, fi, cs],
                       pbk[0][0:E, 0:TB], fi == 0, fi == NFI - 1)
        P.wait("scalar", "eP", vcl)
        prev_hcopy = P.do("scalar", lambda e: e.activation(
            t["hrow"][:], pbk[0][0:E, 0:TB], AF.Identity, bias=t["rb"][:], scale=1.0), inc="eA")
        P.wait("vector", "eA", prev_hcopy)
        vg = P.do("vector", lambda e, cs=cs: e.tensor_add(
            t["hrow"][:], t["hrow"][:], t["gum"][:, cs]), inc="eV")
        P.wait("tensor", "eV", vg)
        for tt in range(4):
            ti = qbi * 4 + tt
            P.wait("tensor", "eV", prev_tr)
            vtr = P.do("tensor", lambda e, tt=tt: e.transpose(
                pbk[1][0:128, 0:E], t["hrow"][:, tt * 128:(tt + 1) * 128], t["ident"][0:E, 0:E]),
                inc="eP")
            P.wait("vector", "eP", vtr)
            P.do("vector", lambda e: e.reduce_max(t["smt3"][:], pbk[1][0:128, 0:E],
                                                  axis=AX.X, negate=True))
            P.do("vector", lambda e: e.tensor_copy(t["smt2"][:], pbk[1][0:128, 0:E]))
            vmx = P.vfence(inc="eV")
            P.wait("scalar", "eV", vmx)
            vex = P.do("scalar", lambda e: e.activation(
                t["smt1"][:], t["smt2"][:], AF.Exp, bias=t["smt3"][:], scale=1.0), inc="eA")
            P.wait("vector", "eA", vex)
            P.vfence()
            P.do("vector", lambda e: e.reduce_sum(t["smt3"][:], t["smt1"][:], axis=AX.X))
            P.vfence()
            P.do("vector", lambda e: e.reciprocal(t["smt4"][:], t["smt3"][:]))
            P.vfence()
            prev_tr = P.do("vector", lambda e, ti=ti: e.tensor_mul(
                t["sm_own"][:, ti, :], t["smt1"][:],
                t["smt4"][:].to_broadcast([128, E])), inc="eV")
    vsm = prev_tr
    P.wait("gpsimd", "eV", vsm)
    P.dma("gpsimd", "dG", d["cc_in"].rearrange("(a p) e -> p a e", p=128), t["sm_own"][:])
    P.wait("gpsimd", "dG")
    P.do("gpsimd", lambda e: e.collective_compute(
        "AllGather", ALU.bypass, replica_groups=[list(range(N_CORES))],
        ins=[d["cc_in"]], outs=[d["cc_out"]]), inc="cc", n=1)
    P.wait("gpsimd", "cc")
    vsmg = P.dma("gpsimd", "dG", t["smg_raw"][:],
                 d["cc_out"].rearrange("(p j) e -> p j e", j=128))
    P.wait("vector", "dG", vsmg)
    vsmg = P.do("vector", lambda e: e.tensor_copy(
        t["smg"].rearrange("p (e j) -> p e j", e=E),
        t["smg_raw"][:].rearrange("p j e -> p e j")), inc="eV")

    W_ = 64
    R = t["brows"]
    b_lo = R[:, 0, :]; b_hi = R[:, 1, :]; b_mid = R[:, 2, :]
    b_cntw = R[:, 3, :]; b_g = R[:, 4, :]; b_d1 = R[:, 5, :]
    b_d2 = R[:, 6, :]; b_tmp = R[:, 7, :]
    smg3 = t["smg"].rearrange("p (e j) -> p e j", e=E)
    bis3 = t["bis"].rearrange("p (e j) -> p e j", e=E)
    P.wait("vector", "eV", vsmg)
    P.do("vector", lambda e: e.memset(b_lo, 0.0))
    P.do("vector", lambda e: e.memset(b_hi, 1.0))
    P.do("vector", lambda e: e.memset(b_mid, 0.5))
    vinit = P.vfence(inc="eV")
    P.wait("tensor", "eV", vinit)
    vmid = P.do("tensor", lambda e: e.matmul(pbk[3][:, 0:512], t["ones_row"][:], b_mid,
                                             start=True, stop=True), inc="eP")
    for _ in range(N_BISECT):
        P.wait("vector", "eP", vmid)
        P.do("vector", lambda e: e.nop(cycle_cnt=2400, nofuse=True))
        pm = pbk[3][:, 0:512]
        mid_b = bass.AP(pm.tensor, pm.offset, [pm.ap[0], [W_, E], [0, 128]])
        P.do("vector", lambda e, b=mid_b: e.tensor_tensor(bis3, smg3, b, ALU.is_gt))
        P.do("vector", lambda e: e.tensor_reduce(t["cnt_pe"][:, 0:E], bis3,
                                                 axis=AX.X, op=ALU.add))
        vind = P.vfence(inc="eV")
        P.wait("tensor", "eV", vind)
        vc1 = P.do("tensor", lambda e: e.matmul(pbk[1][0:1, 0:E], t["ones_col"][:, 0:1],
                                                t["cnt_pe"][:, 0:E], start=True, stop=True),
                   inc="eP")
        P.wait("vector", "eP", vc1)
        P.do("vector", lambda e: e.nop(cycle_cnt=2400, nofuse=True))
        base = pbk[1][0:1, 0:E]
        cnt_b = bass.AP(base.tensor, base.offset, [base.ap[0], [1, E], [0, W_]])
        P.do("vector", lambda e, b=cnt_b: e.tensor_copy(
            b_cntw.rearrange("o (e w) -> o e w", w=W_), b))
        P.do("vector", lambda e: e.tensor_single_scalar(b_g, b_cntw, float(KSEL), ALU.is_ge))
        P.do("vector", lambda e: e.tensor_sub(b_d1, b_mid, b_lo))
        P.do("vector", lambda e: e.tensor_sub(b_d2, b_hi, b_mid))
        P.do("vector", lambda e: e.tensor_mul(b_d1, b_d1, b_g))
        P.do("vector", lambda e: e.tensor_mul(b_d2, b_d2, b_g))
        P.do("vector", lambda e: e.tensor_add(b_lo, b_lo, b_d1))
        P.do("vector", lambda e: e.tensor_add(b_hi, b_mid, b_d2))
        P.do("vector", lambda e: e.tensor_add(b_tmp, b_lo, b_hi))
        P.do("vector", lambda e: e.tensor_scalar_mul(b_mid, b_tmp, 0.5))
        vup = P.vfence(inc="eV")
        P.wait("tensor", "eV", vup)
        vmid = P.do("tensor", lambda e: e.matmul(pbk[3][:, 0:512], t["ones_row"][:], b_mid,
                                                 start=True, stop=True), inc="eP")
    P.wait("vector", "eP", vmid)
    vlo = P.do("tensor", lambda e: e.matmul(pbk[1][:, 0:512], t["ones_row"][:], b_lo,
                                            start=True, stop=True), inc="eP")
    P.wait("vector", "eP", vlo)
    P.do("vector", lambda e: e.nop(cycle_cnt=2400, nofuse=True))
    sm3 = t["sm_own"]
    bis_own = t["bis"][:, 0:NFI * E].rearrange("p (a e) -> p a e", e=E)
    pl = pbk[1][:, 0:512]
    lo_b = bass.AP(pl.tensor, pl.offset, [pl.ap[0], [0, NFI], [W_, E]])
    P.do("vector", lambda e: e.tensor_tensor(bis_own, sm3, lo_b, ALU.is_gt))
    P.vfence()
    P.do("vector", lambda e: e.tensor_mul(bis_own, bis_own, sm3))
    P.vfence()
    P.do("vector", lambda e: e.tensor_reduce(
        t["cnt_pe"][:, 0:NFI], bis_own, axis=AX.X, op=ALU.add))
    vs = P.vfence(inc="eV")
    P.wait("tensor", "eV", vs)
    vtr2 = P.do("tensor", lambda e: e.transpose(
        pbk[1][0:NFI, 0:128], t["cnt_pe"][:, 0:NFI], t["ident"][:]), inc="eP")
    P.wait("vector", "eP", vtr2)
    P.do("vector", lambda e: e.tensor_copy(t["s16"][:], pbk[1][0:NFI, 0:128]))
    vs16 = P.vfence(inc="eV")
    P.wait("gpsimd", "eV", vs16)
    P.dma("gpsimd", "dG", d["srow_tmp"], t["s16"][:])
    P.wait("gpsimd", "dG")
    vsrow = P.dma("gpsimd", "dG", t["s_row"].rearrange("o (a p) -> o a p", p=128),
                  d["srow_tmp"].rearrange("(o a) p -> o a p", o=1))
    P.wait("tensor", "dG", vsrow)
    _barrier(P)
    for _name in DEBUG_TAPS:
        if _name.startswith("sb_"):
            P.dma("gpsimd", "dG", d["dbg_" + _name], t[_name[3:]])
    if DEBUG_TAPS:
        _barrier(P)

    wf = [(t["wf_gh0"], t["wf_gl0"], t["wf_uh0"], t["wf_ul0"]),
          (t["wf_gh1"], t["wf_gl1"], t["wf_uh1"], t["wf_ul1"])]
    wf_free = [0, 0]
    gu_free = [0, 0]
    for dc in range(NDC):
        par = dc % 2
        gh, gl, uh, ul = wf[par]
        P.wait("sync", "eP", wf_free[par])
        csl = slice(dc * 128, (dc + 1) * 128)
        P.dma("sync", "dS", gh[:],
              d["wg_h"][:, :, csl].rearrange("a p c -> p a c"))
        P.dma("sync", "dS", gl[:],
              d["wg_l"][:, :, csl].rearrange("a p c -> p a c"))
        P.dma("sync", "dS", uh[:],
              d["wu_h"][:, :, csl].rearrange("a p c -> p a c"))
        vw = P.dma("sync", "dS", ul[:],
                   d["wu_l"][:, :, csl].rearrange("a p c -> p a c"))
        P.wait("tensor", "dS", vw)
        for tb in range(NQB):
            cs = slice(tb * TB, (tb + 1) * TB)
            gbank = pbk[0 + (tb % 2)]
            ubank = pbk[2 + (tb % 2)]
            P.wait("tensor", "eA", gu_free[tb % 2])
            for fi in range(NFI):
                _mm3(P, gh[:, fi, :], gl[:, fi, :],
                     t["x2h"][:, fi, cs], t["x2l"][:, fi, cs],
                     gbank[:, 0:TB], fi == 0, fi == NFI - 1, inc=False)
            for fi in range(NFI):
                vgu = _mm3(P, uh[:, fi, :], ul[:, fi, :],
                           t["x2h"][:, fi, cs], t["x2l"][:, fi, cs],
                           ubank[:, 0:TB], fi == 0, fi == NFI - 1)
            wf_free[par] = vgu
            P.wait("scalar", "eP", vgu)
            P.do("scalar", lambda e, g=gbank, dc=dc: e.activation(
                t["gev"][:], g[:, 0:TB], AF.Silu, bias=t["bg"][:, dc:dc + 1], scale=1.0))
            vu = P.do("scalar", lambda e, u=ubank, dc=dc: e.activation(
                t["uev"][:], u[:, 0:TB], AF.Identity, bias=t["bu"][:, dc:dc + 1], scale=1.0),
                inc="eA")
            gu_free[tb % 2] = vu
            P.wait("vector", "eA", vu)
            P.wait("vector", "dA")
            P.do("vector", lambda e: e.tensor_mul(t["hfv"][:], t["gev"][:], t["uev"][:]))
            P.do("vector", lambda e: e.tensor_copy(t["hh_ev"][:], t["hfv"][:]))
            vh_ = P.do("vector", lambda e: e.tensor_sub(t["hl_ev"][:], t["hfv"][:],
                                                        t["hh_ev"][:]), inc="eV")
            P.wait("scalar", "eV", vh_)
            P.dma("scalar", "dA", d["hbuf_h"][tb, :, dc, :], t["hh_ev"][:])
            P.dma("scalar", "dA", d["hbuf_l"][tb, :, dc, :], t["hl_ev"][:])
    _barrier(P)

    NA = 22
    NB2 = NDC - NA
    wdt = [(t["wd_h0"], t["wd_l0"]), (t["wd_h1"], t["wd_l1"])]
    wd_free = [0, 0]
    ev_free = [0, 0]
    out_store = [0, 0]
    st_guard = [0, 0]
    ypg = [0] * NFI
    hA_free = 0
    hB_free = 0
    hld = {"A": 0, "B": 0}

    def load_half(tb, half):
        lo_, hi_ = (0, NA) if half == "A" else (NA, NDC)
        P.wait("gpsimd", "eP", hA_free if half == "A" else hB_free)
        P.dma("gpsimd", "dG", t["hsp_h"][:, lo_:hi_, :], d["hbuf_h"][tb, :, lo_:hi_, :])
        hld[half] = P.dma("gpsimd", "dG", t["hsp_l"][:, lo_:hi_, :],
                          d["hbuf_l"][tb, :, lo_:hi_, :])

    load_half(0, "A")
    load_half(0, "B")
    for tb in range(NQB):
        vsb = P.do("tensor", lambda e, tb=tb: e.matmul(
            pbk[6][:, 0:TB], t["ones_row"][:], t["s_row"][:, tb * TB:(tb + 1) * TB],
            start=True, stop=True), inc="eP")
        P.wait("vector", "eP", vsb)
        vsbc = P.do("vector", lambda e: e.tensor_copy(t["s_bc"][:], pbk[6][:, 0:TB]), inc="eV")
        vhA, vhB = hld["A"], hld["B"]
        for half in ("A", "B"):
            n_dc = NA if half == "A" else NB2
            base = 0 if half == "A" else NA
            for fo in range(NFI):
                par = fo % 2
                wh, wl = wdt[par]
                P.wait("sync", "eP", wd_free[par])
                P.dma("sync", "dS", wh[:, 0:n_dc, :], d["wd_h"][fo, :, base:base + n_dc, :])
                vw = P.dma("sync", "dS", wl[:, 0:n_dc, :],
                           d["wd_l"][fo, :, base:base + n_dc, :])
                if half == "B":
                    P.wait("scalar", "eV", st_guard[par])
                    vst_ = P.dma("scalar", "dA", [t["st0"], t["st1"]][par][:],
                                 d["stream"][tb, fo, :, :])
                P.wait("tensor", "dS", vw)
                P.wait("tensor", "dG", vhA if half == "A" else vhB)
                P.wait("tensor", "eA", ev_free[par])
                bank = pbk[0 + par]
                for j in range(n_dc):
                    vy = _mm3(P, wh[:, j, :], wl[:, j, :],
                              t["hsp_h"][:, base + j, :], t["hsp_l"][:, base + j, :],
                              bank[:, 0:TB], j == 0, j == n_dc - 1)
                wd_free[par] = vy
                P.wait("scalar", "eP", vy)
                if half == "A":
                    P.wait("scalar", "eV", ypg[fo])
                    vyb = P.do("scalar", lambda e, b=bank, fo=fo: e.activation(
                        t["y_part"][:, fo, :], b[:, 0:TB], AF.Copy,
                        bias=0.0, scale=1.0), inc="eA")
                    ev_free[par] = vyb
                else:
                    vyb = P.do("scalar", lambda e, b=bank, fo=fo, p=par: e.activation(
                        [t["yt0"], t["yt1"]][p][:], b[:, 0:TB], AF.Identity,
                        bias=t["bd"][:, fo:fo + 1], scale=1.0), inc="eA")
                    ev_free[par] = vyb
                    P.wait("vector", "eA", vyb)
                    P.wait("vector", "dA", vst_)
                    P.wait("vector", "dA", out_store[par])
                    yt = [t["yt0"], t["yt1"]][par]
                    ot = [t["ot0"], t["ot1"]][par]
                    stt = [t["st0"], t["st1"]][par]
                    P.do("vector", lambda e, yt=yt, fo=fo: e.tensor_add(
                        yt[:], yt[:], t["y_part"][:, fo, :]))
                    P.do("vector", lambda e, yt=yt: e.tensor_mul(yt[:], yt[:], t["s_bc"][:]))
                    vout = P.do("vector", lambda e, yt=yt, ot=ot, s=stt: e.tensor_add(
                        ot[:], yt[:], s[:]), inc="eV")
                    ypg[fo] = vout
                    st_guard[par] = vout
                    P.wait("scalar", "eV", vout)
                    out_store[par] = P.dma("scalar", "dA", out_ap[tb, fo, :, :], ot[:])
            if half == "A":
                hA_free = vy
                if tb + 1 < NQB:
                    load_half(tb + 1, "A")
            else:
                hB_free = vy
                if tb + 1 < NQB:
                    load_half(tb + 1, "B")
    if DEBUG_TAPS:
        _barrier(P)
        for name in DEBUG_TAPS:
            if not name.startswith("sb_"):
                P.dma("sync", "dS", d["dbg_" + name], d[name])
    P.wait("sync", "dS")



def _stage_inputs(L0_attn_x, wq, wk, wv, wo, rw, rb, wg, bgv, wu, buv, wdn, bdv):
    x = np.ascontiguousarray(np.asarray(L0_attn_x, dtype=np.float32))
    C = _host_constants()
    wqT = np.ascontiguousarray(np.asarray(wq, np.float32).T)
    wkT = np.ascontiguousarray(np.asarray(wk, np.float32).T)
    wvT = np.ascontiguousarray(np.asarray(wv, np.float32).T)
    woT = np.ascontiguousarray(np.asarray(wo, np.float32).T)
    wg_p = np.zeros((D, DFFP), np.float32); wg_p[:, :DFF] = np.asarray(wg, np.float32)
    wu_p = np.zeros((D, DFFP), np.float32); wu_p[:, :DFF] = np.asarray(wu, np.float32)
    wd_p = np.zeros((DFFP, D), np.float32); wd_p[:DFF, :] = np.asarray(wdn, np.float32)
    bg_p = np.zeros((DFFP,), np.float32); bg_p[:DFF] = np.asarray(bgv, np.float32)
    bu_p = np.zeros((DFFP,), np.float32); bu_p[:DFF] = np.asarray(buv, np.float32)
    bg_t = np.ascontiguousarray(bg_p.reshape(NDC, 128).T)
    bu_t = np.ascontiguousarray(bu_p.reshape(NDC, 128).T)
    bd_t = np.ascontiguousarray(np.asarray(bdv, np.float32).reshape(NFI, 128).T)
    cos_k = np.ascontiguousarray(C["cos"].T)
    sin_k = np.ascontiguousarray(C["sin"].T)
    ones_col = np.ones((128, 1), np.float32)
    ones_row = np.ones((1, 128), np.float32)
    rb2 = np.ascontiguousarray(np.asarray(rb, np.float32).reshape(E, 1))
    rwf = np.ascontiguousarray(np.asarray(rw, np.float32))

    in_maps = []
    for c in range(N_CORES):
        b = c // 2
        own = OWN_BLOCKS[c % 2]
        own_cols = np.concatenate([np.arange(bi * TB, (bi + 1) * TB) for bi in own])
        xt = np.ascontiguousarray(x[b].T)
        in_maps.append({
            "xt_full": xt,
            "xt_own": np.ascontiguousarray(xt[:, own_cols]),
            "wqT": wqT, "wkT": wkT, "wvT": wvT, "woT": woT,
            "wg": wg_p, "wu": wu_p, "wd": wd_p,
            "bg": bg_t, "bu": bu_t, "bd": bd_t,
            "rw": rwf, "rb": rb2,
            "cos_k": cos_k, "sin_k": sin_k,
            "cos_q": np.ascontiguousarray(cos_k[:, own_cols]),
            "sin_q": np.ascontiguousarray(sin_k[:, own_cols]),
            "gumbel_own": np.ascontiguousarray(C["gumbel"][b * S:(b + 1) * S][own_cols].T),
            "masks": _attn_masks(own),
            "permT": C["permT"], "ident": C["ident"],
            "ones_col": ones_col, "ones_row": ones_row,
        })
    return in_maps


def kernel(L0_attn_x, L0_attn_wq, L0_attn_wk, L0_attn_wv, L0_attn_wo,
           L0_ffn_router_w, L0_ffn_router_b, L0_ffn_w_gate, L0_ffn_b_gate,
           L0_ffn_w_up, L0_ffn_b_up, L0_ffn_w_down, L0_ffn_b_down):
    in_maps = _stage_inputs(L0_attn_x, L0_attn_wq, L0_attn_wk, L0_attn_wv,
                            L0_attn_wo, L0_ffn_router_w, L0_ffn_router_b,
                            L0_ffn_w_gate, L0_ffn_b_gate, L0_ffn_w_up,
                            L0_ffn_b_up, L0_ffn_w_down, L0_ffn_b_down)
    nc = build_program()
    res = run_bass_kernel_spmd(nc, in_maps, core_ids=list(range(N_CORES)),
                               **globals().get("_RUN_KW", {}))
    globals()["_LAST_RES"] = res

    out = np.empty((B, S, D), dtype=np.float32)
    for c in range(N_CORES):
        b = c // 2
        own = OWN_BLOCKS[c % 2]
        shard = res.results[c]["out_fm"]
        for qbi, bi in enumerate(own):
            out[b, bi * TB:(bi + 1) * TB, :] = shard[qbi].reshape(D, TB).T
    return out
